# version 05 (final)
# baseline (speedup 1.0000x reference)
import contextlib
import numpy as np
import ml_dtypes

import concourse.bass as bass
import concourse.mybir as mybir
from concourse.bass_utils import run_bass_kernel_spmd

F32 = mybir.dt.float32
BF16 = mybir.dt.bfloat16
I32 = mybir.dt.int32
AF = mybir.ActivationFunctionType
ALU = mybir.AluOpType
AX = mybir.AxisListType

FLAG_VAL = 0x5A5A1234
NEG_BIG = -1.0e30


class Cfg:
    def __init__(self, D, F, NH, n_tt_own, t_grp, qg_tau):
        self.D = D
        self.F = F
        self.NH = NH
        self.HD = 128
        assert NH * self.HD == D
        self.ND = D // 128
        self.FW = min(512, F)
        self.NF = F // self.FW
        self.NPSI = F // 128
        self.OW = min(512, D)
        self.OG = D // self.OW
        self.QKVO = 3 * D // 128
        self.n_tt = n_tt_own
        self.T = n_tt_own * 128
        self.S = 2 * self.T
        self.t_grp = t_grp
        self.n_grp = n_tt_own // t_grp
        self.qg_tau = qg_tau
        self.n_qg = n_tt_own // qg_tau
        self.TGW = min(512, self.T)
        self.NTG = self.T // self.TGW
        self.att_c = float(1.0 / np.sqrt(np.float32(self.HD)))


FULL = Cfg(D=2048, F=8192, NH=16, n_tt_own=16, t_grp=8, qg_tau=4)
TINY = Cfg(D=256, F=1024, NH=2, n_tt_own=4, t_grp=2, qg_tau=2)


class B:

    ENGS = ("pe", "dve", "act", "pool", "sync")

    def __init__(self, nc):
        self.nc = nc
        self.eng = {"pe": nc.tensor, "dve": nc.vector, "act": nc.scalar,
                    "pool": nc.gpsimd, "sync": nc.sync}
        self.sem = {}
        self.cnt = {}
        self.waited = {}

    def mksem(self, stack, *names):
        for n in names:
            self.sem[n] = stack.enter_context(self.nc.semaphore(n))
            self.cnt[n] = 0

    def inc(self, instr, name, k=1):
        instr.then_inc(self.sem[name], k)
        self.cnt[name] += k
        return self.cnt[name]

    def wait(self, eng, name, val=None):
        if val is None:
            val = self.cnt[name]
        if val <= 0:
            return
        key = (eng, name)
        if self.waited.get(key, 0) >= val:
            return
        self.eng[eng].wait_ge(self.sem[name], val)
        self.waited[key] = val

    def gate(self, engs=None, skip=("FLG",)):
        for e in (engs or self.ENGS):
            for s in self.sem:
                if s not in skip:
                    self.wait(e, s)


def build(cfg: Cfg, only_ffn1=False):
    nc = bass.Bass("TRN2", target_bir_lowering=False, debug=False,
                   detect_race_conditions=False)
    D, F, NH, ND = cfg.D, cfg.F, cfg.NH, cfg.ND
    T, n_tt = cfg.T, cfg.n_tt

    x0 = nc.dram_tensor("x0", [T, D], F32, kind="ExternalInput").ap()
    wqkvT = nc.dram_tensor("wqkvT", [D, 3 * D], F32, kind="ExternalInput").ap()
    woT = nc.dram_tensor("woT", [D, D], F32, kind="ExternalInput").ap()
    wfcT = nc.dram_tensor("wfcT", [D, F], F32, kind="ExternalInput").ap()
    wprojT = nc.dram_tensor("wprojT", [F, D], F32, kind="ExternalInput").ap()
    ident = nc.dram_tensor("ident", [128, 128], BF16, kind="ExternalInput").ap()
    identf = nc.dram_tensor("identf", [128, 128], F32, kind="ExternalInput").ap()
    onesf = nc.dram_tensor("onesf", [1, 128], F32, kind="ExternalInput").ap()
    mask2 = nc.dram_tensor("mask2", [128, 256], F32, kind="ExternalInput").ap()
    slot_in = nc.dram_tensor("slot", [1, 2], I32, kind="ExternalInput").ap()
    cconst = nc.dram_tensor("cconst", [128, 4], F32, kind="ExternalInput").ap()

    y = nc.dram_tensor("y", [T, D], F32, kind="ExternalOutput").ap()

    fcT_b = nc.dram_tensor("fcT_b", [D, F], BF16).ap()
    projT_b = nc.dram_tensor("projT_b", [F, D], BF16).ap()
    res1 = nc.dram_tensor("res1", [T, D], F32).ap()
    res2 = nc.dram_tensor("res2", [T, D], F32).ap()
    kv_sh = nc.dram_tensor("kv_sh", [2, 2 * NH * 128, T], BF16,
                           addr_space="Shared").ap()
    kv_loc = nc.dram_tensor("kv_loc", [2 * NH * 128, T], BF16).ap()
    kv_fl = nc.dram_tensor("kv_fl", [2, 1], I32, addr_space="Shared").ap()

    stack = contextlib.ExitStack()
    _uid = [0]

    def _nm(base):
        _uid[0] += 1
        return f"{base}_{_uid[0]}"

    with stack:
        b = B(nc)
        b.mksem(stack, "LD", "LD2", "LDG", "STG", "FLG",
                "PE", "DV", "AC", "PL", "MS")

        def sb(name, shape, dt):
            return stack.enter_context(nc.sbuf_tensor(name, shape, dt))

        ident_sb = sb("ident_sb", [128, 128], BF16)
        identf_sb = sb("identf_sb", [128, 128], F32)
        onesf_sb = sb("onesf_sb", [1, 128], F32)
        mask_sb = sb("mask_sb", [128, 256], F32)
        slot_sb = sb("slot_sb", [1, 2], I32)
        flag_sb = sb("flag_sb", [1, 1], I32)
        flin_sb = sb("flin_sb", [1, 1], I32)
        gl_sb = sb("gl_sb", [128, 8], F32)
        cc_sb = sb("cc_sb", [128, 4], F32)
        scr_sb = sb("scr_sb", [128, 512], BF16)
        scrf_sb = sb("scrf_sb", [128, 128], F32)
        scrg_sb = sb("scrg_sb", [128, 16], F32)

        b.inc(nc.sync.dma_start(out=ident_sb[:], in_=ident[:]), "LD", 16)
        b.inc(nc.sync.dma_start(out=identf_sb[:], in_=identf[:]), "LD", 16)
        b.inc(nc.sync.dma_start(out=onesf_sb[:], in_=onesf[:]), "LD", 16)
        b.inc(nc.sync.dma_start(out=mask_sb[:], in_=mask2[:]), "LD", 16)
        b.inc(nc.sync.dma_start(out=slot_sb[:], in_=slot_in[:]), "LD", 16)
        b.inc(nc.sync.dma_start(out=cc_sb[:], in_=cconst[:]), "LD", 16)
        for _e in B.ENGS:
            b.wait(_e, "LD")
        myslot = nc.values_load(slot_sb[0:1, 0:1],
                                engines=[mybir.EngineType.Pool],
                                min_val=0, max_val=1,
                                skip_runtime_bounds_check=True)
        otslot = nc.values_load(slot_sb[0:1, 1:2],
                                engines=[mybir.EngineType.Pool],
                                min_val=0, max_val=1,
                                skip_runtime_bounds_check=True)

        def dvop(fn):
            b.inc(fn(), "DV")
            b.wait("dve", "DV")

        def pe_fence(ps_ap):
            nc.tensor.drain()
            b.inc(nc.tensor.matmul(
                out=ps_ap, lhsT=ident_sb[:], rhs=ident_sb[:, 0:128],
                start=True, stop=True), "PE")

        def weight_prep():
            with (
                nc.sbuf_tensor(_nm("wt0"), [128, F], F32) as wt0,
                nc.sbuf_tensor(_nm("wt1"), [128, F], F32) as wt1,
                nc.sbuf_tensor(_nm("slots"), [128, 96], F32) as slots,
                nc.sbuf_tensor(_nm("tot"), [1, 2], F32) as tot,
                nc.sbuf_tensor(_nm("nb0"), [128, F], BF16) as nb0,
                nc.sbuf_tensor(_nm("tb0"), [128, F], BF16) as tb0,
                nc.sbuf_tensor(_nm("tb1"), [128, F], BF16) as tb1,
                nc.psum_tensor(_nm("bc_ps"), [128, 1], F32) as bc_ps,
            ):
                wts = [wt0, wt1]
                tbs = [tb0, tb1]

                def scan(w_ap, nrow, ncol, slot0):
                    dv_done = []
                    b.wait("sync", "DV")
                    lda = {}

                    def ld(r):
                        if r >= 2:
                            b.wait("sync", "DV", dv_done[r - 2])
                        b.inc(nc.sync.dma_start(
                            out=wts[r % 2][:, 0:ncol],
                            in_=w_ap[r * 128:(r + 1) * 128, :]), "LD", 16)
                        lda[r] = b.cnt["LD"]

                    ld(0)
                    for r in range(nrow):
                        buf = wts[r % 2]
                        if r + 1 < nrow:
                            ld(r + 1)
                        else:
                            lda[r + 1] = b.inc(nc.sync.dma_start(
                                out=scrf_sb[:, 0:128],
                                in_=w_ap[0:128, 0:128]), "LD", 16)
                        b.wait("dve", "LD", lda[r + 1])
                        dv_done.append(b.inc(nc.vector.tensor_reduce(
                            out=slots[:, slot0 + r:slot0 + r + 1],
                            in_=buf[:, 0:ncol], axis=AX.X, op=ALU.add,
                            apply_absolute_value=True), "DV"))
                    b.wait("dve", "DV")
                    b.inc(nc.vector.tensor_reduce(
                        out=slots[:, slot0 + nrow:slot0 + nrow + 1],
                        in_=slots[:, slot0:slot0 + nrow], axis=AX.X,
                        op=ALU.add), "DV")

                scan(wfcT, D // 128, F, 0)
                scan(wprojT, F // 128, D, D // 128 + 1)
                c_fc = D // 128
                c_pr = D // 128 + 1 + F // 128
                b.wait("pool", "DV")
                b.inc(nc.gpsimd.tensor_reduce(
                    out=tot[0:1, 0:1], in_=slots[:, c_fc:c_fc + 1],
                    axis=AX.C, op=ALU.add), "PL")
                b.inc(nc.gpsimd.tensor_reduce(
                    out=tot[0:1, 1:2], in_=slots[:, c_pr:c_pr + 1],
                    axis=AX.C, op=ALU.add), "PL")
                b.wait("pe", "PL")
                b.wait("pe", "LD")
                for j, (nel, base) in enumerate(((float(D) * F, 0),
                                                 (float(F) * D, 3))):
                    b.wait("pe", "DV")
                    b.inc(nc.tensor.matmul(
                        out=bc_ps[:], lhsT=onesf_sb[:],
                        rhs=tot[0:1, j:j + 1], start=True, stop=True), "PE")
                    b.wait("dve", "PE")
                    dvop(lambda: nc.vector.tensor_scalar(
                        out=gl_sb[:, base:base + 1], in0=bc_ps[:],
                        scalar1=1.0 / nel, scalar2=None, op0=ALU.mult))
                    dvop(lambda: nc.vector.tensor_scalar(
                        out=gl_sb[:, base + 1:base + 2],
                        in0=gl_sb[:, base:base + 1],
                        scalar1=1e-6, scalar2=0.5, op0=ALU.add,
                        op1=ALU.mult))
                    dvop(lambda: nc.vector.tensor_scalar(
                        out=gl_sb[:, base + 2:base + 3],
                        in0=gl_sb[:, base + 1:base + 2],
                        scalar1=-1.0, scalar2=None, op0=ALU.mult))
                dvop(lambda: nc.vector.tensor_scalar(
                    out=gl_sb[:, 6:7], in0=gl_sb[:, 3:4],
                    scalar1=0.5, scalar2=None, op0=ALU.mult))

                def ternarize(w_ap, scratch, nrow, ncol, base):
                    thr = gl_sb[:, base + 1:base + 2]
                    nthr = gl_sb[:, base + 2:base + 3]
                    b.wait("sync", "DV")
                    st_done = []
                    dv_done = []
                    lda = {}

                    def ld(r):
                        if r >= 2:
                            b.wait("sync", "DV", dv_done[r - 2][0]
                                   if isinstance(dv_done[r - 2], tuple)
                                   else dv_done[r - 2])
                        b.inc(nc.sync.dma_start(
                            out=wts[r % 2][:, 0:ncol],
                            in_=w_ap[r * 128:(r + 1) * 128, :]), "LD", 16)
                        lda[r] = b.cnt["LD"]

                    ld(0)
                    for r in range(nrow):
                        buf = wts[r % 2]
                        tb = tbs[r % 2]
                        if r + 1 < nrow:
                            ld(r + 1)
                        else:
                            lda[r + 1] = b.inc(nc.sync.dma_start(
                                out=scrf_sb[:, 0:128],
                                in_=w_ap[0:128, 0:128]), "LD", 16)
                        b.wait("dve", "LD", lda[r + 1])
                        if r >= 2:
                            b.wait("dve", "STG", st_done[r - 2])
                        b.inc(nc.vector.tensor_scalar(
                            out=nb0[:, 0:ncol], in0=buf[:, 0:ncol],
                            scalar1=nthr, scalar2=None, op0=ALU.is_lt), "DV")
                        b.wait("dve", "DV")
                        dv_done.append(b.inc(nc.vector.scalar_tensor_tensor(
                            out=tb[:, 0:ncol], in0=buf[:, 0:ncol],
                            scalar=thr, in1=nb0[:, 0:ncol],
                            op0=ALU.is_gt, op1=ALU.subtract), "DV"))
                        b.wait("pool", "DV")
                        st_done.append(b.inc(nc.gpsimd.dma_start(
                            out=scratch[r * 128:(r + 1) * 128, :],
                            in_=tb[:, 0:ncol]), "STG", 16))

                ternarize(wfcT, fcT_b, D // 128, F, 0)
                ternarize(wprojT, projT_b, F // 128, D, 3)
                b.wait("pool", "STG")
                b.inc(nc.gpsimd.dma_start(
                    out=scr_sb[:, 0:16], in_=fcT_b[0:128, 0:16]), "LDG", 16)
                b.inc(nc.gpsimd.dma_start(
                    out=scr_sb[:, 16:32], in_=projT_b[0:128, 0:16]),
                    "LDG", 16)

        weight_prep()
        b.gate()

        def ffn(x_src, x_dst):
            TG = cfg.t_grp
            usz = max(TG * ND * 128, F)
            with (
                nc.sbuf_tensor(_nm("g_sb"), [128, TG * F], BF16) as g_sb,
                nc.sbuf_tensor(_nm("xhT_u"), [128, usz], BF16) as xhT_u,
                nc.sbuf_tensor(_nm("xin0"), [128, D], F32) as xin0,
                nc.sbuf_tensor(_nm("xin1"), [128, D], F32) as xin1,
                nc.sbuf_tensor(_nm("xh0"), [128, D], BF16) as xh0,
                nc.sbuf_tensor(_nm("st1"), [128, 16 * TG], F32) as st1,
                nc.sbuf_tensor(_nm("wf0"), [128, 512], BF16) as wf0,
                nc.sbuf_tensor(_nm("wf1"), [128, 512], BF16) as wf1,
                nc.sbuf_tensor(_nm("wf2"), [128, 512], BF16) as wf2,
                nc.sbuf_tensor(_nm("wf3"), [128, 512], BF16) as wf3,
                nc.sbuf_tensor(_nm("xs0"), [128, cfg.OW], F32) as xs0,
                nc.sbuf_tensor(_nm("xs1"), [128, cfg.OW], F32) as xs1,
                nc.sbuf_tensor(_nm("og0"), [128, cfg.OW], F32) as og0,
                nc.sbuf_tensor(_nm("og1"), [128, cfg.OW], F32) as og1,
            ):
                xins = [xin0, xin1]
                wfs = [wf0, wf1, wf2, wf3]
                xss = [xs0, xs1]
                ogs = [og0, og1]
                xhT = xhT_u
                sq = xhT_u

                def sv(k):
                    return st1.rearrange("p (t k) -> p t k", k=16)[
                        :, :, k:k + 1].rearrange("p t k -> p (t k)")

                s1, s2 = sv(0), sv(1)
                mx, mn = sv(2), sv(3)
                nm, sp = sv(4), sv(5)
                g1, ng1 = sv(6), sv(7)
                tmpa, tmpb, tmpc = sv(8), sv(9), sv(10)

                for grp in range(cfg.n_grp):
                    tt0 = grp * TG
                    dv_use = []
                    lda = {}

                    def xld(ti):
                        if ti >= 2:
                            b.wait("sync", "DV", dv_use[ti - 2])
                        else:
                            b.wait("sync", "DV")
                        b.inc(nc.sync.dma_start(
                            out=xins[ti % 2][:],
                            in_=x_src[(tt0 + ti) * 128:
                                      (tt0 + ti + 1) * 128, :]), "LD", 16)
                        lda[ti] = b.cnt["LD"]

                    xld(0)
                    for ti in range(TG):
                        tau = tt0 + ti
                        buf = xins[ti % 2]
                        if ti + 1 < TG:
                            xld(ti + 1)
                        else:
                            lda[ti + 1] = b.inc(nc.sync.dma_start(
                                out=scrf_sb[:, 0:64],
                                in_=x_src[0:128, 0:64]), "LD", 16)
                        b.wait("dve", "LD", lda[ti + 1])
                        s = ti * 16
                        b.inc(nc.vector.tensor_reduce(
                            out=st1[:, s:s + 1], in_=buf[:], axis=AX.X,
                            op=ALU.add), "DV")
                        b.inc(nc.vector.scalar_tensor_tensor(
                            out=sq[:, 0:D], in0=buf[:], scalar=1.0,
                            in1=buf[:], op0=ALU.mult, op1=ALU.mult,
                            accum_out=st1[:, s + 1:s + 2]), "DV")
                        b.inc(nc.vector.tensor_reduce(
                            out=st1[:, s + 2:s + 3], in_=buf[:], axis=AX.X,
                            op=ALU.max), "DV")
                        dv_use.append(b.inc(nc.vector.tensor_reduce(
                            out=st1[:, s + 3:s + 4], in_=buf[:], axis=AX.X,
                            op=ALU.min), "DV"))
                    b.wait("dve", "DV")
                    dvop(lambda: nc.vector.tensor_scalar(
                        out=tmpa, in0=s1, scalar1=1.0 / D, scalar2=None,
                        op0=ALU.mult))
                    dvop(lambda: nc.vector.tensor_scalar(
                        out=tmpb, in0=s2, scalar1=1.0 / D, scalar2=None,
                        op0=ALU.mult))
                    dvop(lambda: nc.vector.tensor_tensor(
                        out=tmpc, in0=tmpa, in1=tmpa, op=ALU.mult))
                    dvop(lambda: nc.vector.tensor_tensor(
                        out=tmpb, in0=tmpb, in1=tmpc, op=ALU.subtract))
                    b.wait("act", "DV")
                    b.inc(nc.scalar.activation(
                        out=tmpc, in_=tmpb, func=AF.Sqrt, bias=cc_sb[:, 0:1],
                        scale=1.0), "AC")
                    dvop(lambda: nc.vector.tensor_scalar(
                        out=sp, in0=tmpb, scalar1=1e-5, scalar2=None,
                        op0=ALU.add))
                    dvop(lambda: nc.vector.reciprocal(out=sp, in_=sp))
                    dvop(lambda: nc.vector.tensor_tensor(
                        out=sp, in0=tmpb, in1=sp, op=ALU.mult))
                    b.wait("act", "DV")
                    b.inc(nc.scalar.activation(
                        out=sp, in_=sp, func=AF.Sqrt, bias=cc_sb[:, 1:2],
                        scale=1.0), "AC")
                    b.wait("dve", "AC")
                    dvop(lambda: nc.vector.tensor_tensor(
                        out=sp, in0=sp, in1=tmpc, op=ALU.mult))
                    dvop(lambda: nc.vector.reciprocal(out=sp, in_=sp))
                    dvop(lambda: nc.vector.tensor_tensor(
                        out=mx, in0=mx, in1=tmpa, op=ALU.subtract))
                    dvop(lambda: nc.vector.tensor_tensor(
                        out=mn, in0=tmpa, in1=mn, op=ALU.subtract))
                    dvop(lambda: nc.vector.tensor_tensor(
                        out=mx, in0=mx, in1=mn, op=ALU.max))
                    dvop(lambda: nc.vector.tensor_tensor(
                        out=g1, in0=sp, in1=mx, op=ALU.mult))
                    dvop(lambda: nc.vector.tensor_scalar(
                        out=ng1, in0=g1, scalar1=-1.0, scalar2=None,
                        op0=ALU.mult))
                    dvop(lambda: nc.vector.tensor_scalar(
                        out=nm, in0=tmpa, scalar1=-1.0, scalar2=None,
                        op0=ALU.mult))
                    with nc.psum_tensor(_nm("trp"), [128, 1024], BF16) as trp:
                        lda2 = {}

                        def xld2(ti):
                            b.wait("sync", "DV")
                            b.inc(nc.sync.dma_start(
                                out=xins[ti % 2][:],
                                in_=x_src[(tt0 + ti) * 128:
                                          (tt0 + ti + 1) * 128, :]),
                                "LD", 16)
                            lda2[ti] = b.cnt["LD"]

                        xld2(0)
                        for ti in range(TG):
                            tau = tt0 + ti
                            buf = xins[ti % 2]
                            if ti + 1 < TG:
                                xld2(ti + 1)
                            else:
                                lda2[ti + 1] = b.inc(nc.sync.dma_start(
                                    out=scrf_sb[:, 0:64],
                                    in_=x_src[0:128, 0:64]), "LD", 16)
                            b.wait("dve", "LD", lda2[ti + 1])
                            s = ti * 16
                            b.inc(nc.vector.tensor_scalar(
                                out=xh0[:], in0=buf[:],
                                scalar1=st1[:, s + 4:s + 5],
                                scalar2=st1[:, s + 5:s + 6],
                                op0=ALU.add, op1=ALU.mult), "DV")
                            b.wait("dve", "DV")
                            b.inc(nc.vector.tensor_scalar(
                                out=xh0[:], in0=xh0[:],
                                scalar1=st1[:, s + 6:s + 7],
                                scalar2=st1[:, s + 7:s + 8],
                                op0=ALU.min, op1=ALU.max), "DV")
                            b.wait("pe", "DV")
                            for dd in range(ND):
                                b.inc(nc.tensor.transpose(
                                    out=trp[:, (dd % 8) * 128:
                                            (dd % 8) * 128 + 128],
                                    in_=xh0[:, dd * 128:dd * 128 + 128],
                                    identity=ident_sb[:]), "PE")
                                if dd % 8 == 7 or dd == ND - 1:
                                    nblk = dd % 8 + 1
                                    c0 = (dd - nblk + 1) * 128
                                    b.wait("dve", "PE")
                                    b.inc(nc.vector.tensor_copy(
                                        out=xhT[:, ti * ND * 128 + c0:
                                                ti * ND * 128 + c0 +
                                                nblk * 128],
                                        in_=trp[:, 0:nblk * 128]), "DV")
                                    if dd != ND - 1:
                                        b.wait("pe", "DV")
                    with contextlib.ExitStack() as pstk:
                        fps = [pstk.enter_context(nc.psum_tensor(
                            _nm(f"fc{i}"), [128, cfg.FW], F32))
                            for i in range(TG)]
                        ac_evac = {}
                        pe_after = {}
                        ld_w = {}
                        n_it = cfg.NF * ND
                        NSL = len(wfs)

                        def fc_load(it):
                            fi, dd = it // ND, it % ND
                            wbuf = wfs[it % NSL]
                            if it >= NSL:
                                b.wait("sync", "PE", pe_after[it - NSL])
                            b.inc(nc.sync.dma_start(
                                out=wbuf[:, 0:cfg.FW],
                                in_=fcT_b[dd * 128:(dd + 1) * 128,
                                          fi * cfg.FW:
                                          (fi + 1) * cfg.FW]), "LD", 16)
                            ld_w[it] = b.cnt["LD"]

                        for it0 in range(min(NSL, n_it)):
                            fc_load(it0)
                        b.wait("pe", "DV")
                        pe_fence(fps[0][:, 0:128])
                        for fi in range(cfg.NF):
                            for dd in range(ND):
                                it = fi * ND + dd
                                wbuf = wfs[it % NSL]
                                b.wait("pe", "LD",
                                       ld_w.get(it + 1, ld_w[it]))
                                b.wait("pe", "LD", ld_w[it])
                                b.wait("pe", "DV")
                                for ti in range(TG):
                                    if dd == 0 and fi > 0:
                                        b.wait("pe", "AC",
                                               ac_evac[(fi - 1, ti)])
                                    b.inc(nc.tensor.matmul(
                                        out=fps[ti][:],
                                        lhsT=xhT[:, (ti * ND + dd) * 128:
                                                 (ti * ND + dd) * 128 +
                                                 128],
                                        rhs=wbuf[:, 0:cfg.FW],
                                        start=(dd == 0),
                                        stop=(dd == ND - 1)), "PE")
                                pe_after[it] = b.cnt["PE"]
                                if it + NSL < n_it:
                                    fc_load(it + NSL)
                                elif it == n_it - 1:
                                    ld_w[it + 1] = b.inc(nc.sync.dma_start(
                                        out=scr_sb[:, 0:cfg.FW],
                                        in_=fcT_b[0:128, 0:cfg.FW]), "LD", 16)
                            b.wait("act", "PE")
                            for ti in range(TG):
                                ac_evac[(fi, ti)] = b.inc(
                                    nc.scalar.activation(
                                        out=g_sb[:, ti * F + fi * cfg.FW:
                                                 ti * F +
                                                 (fi + 1) * cfg.FW],
                                        in_=fps[ti][:], func=AF.Gelu,
                                        bias=0.0, scale=gl_sb[:, 0:1]),
                                    "AC")
                    b.wait("dve", "AC")
                    for ti in range(TG):
                        gv = g_sb[:, ti * F:(ti + 1) * F]
                        s = ti * 16
                        b.inc(nc.vector.tensor_reduce(
                            out=st1[:, s:s + 1], in_=gv, axis=AX.X,
                            op=ALU.add), "DV")
                        b.inc(nc.vector.scalar_tensor_tensor(
                            out=sq[:, 0:F], in0=gv, scalar=1.0, in1=gv,
                            op0=ALU.mult, op1=ALU.mult,
                            accum_out=st1[:, s + 1:s + 2]), "DV")
                        b.inc(nc.vector.tensor_reduce(
                            out=st1[:, s + 2:s + 3], in_=gv, axis=AX.X,
                            op=ALU.max), "DV")
                        b.inc(nc.vector.tensor_reduce(
                            out=st1[:, s + 3:s + 4], in_=gv, axis=AX.X,
                            op=ALU.min), "DV")
                    b.wait("dve", "DV")
                    dvop(lambda: nc.vector.tensor_scalar(
                        out=tmpa, in0=s1, scalar1=1.0 / F, scalar2=None,
                        op0=ALU.mult))
                    dvop(lambda: nc.vector.tensor_scalar(
                        out=tmpb, in0=s2, scalar1=1.0 / F, scalar2=None,
                        op0=ALU.mult))
                    dvop(lambda: nc.vector.tensor_tensor(
                        out=tmpc, in0=tmpa, in1=tmpa, op=ALU.mult))
                    dvop(lambda: nc.vector.tensor_tensor(
                        out=tmpb, in0=tmpb, in1=tmpc, op=ALU.subtract))
                    b.wait("act", "DV")
                    b.inc(nc.scalar.activation(
                        out=sp, in_=tmpb, func=AF.Sqrt, bias=cc_sb[:, 1:2],
                        scale=1.0), "AC")
                    b.wait("dve", "AC")
                    dvop(lambda: nc.vector.reciprocal(out=sp, in_=sp))
                    dvop(lambda: nc.vector.tensor_tensor(
                        out=mx, in0=mx, in1=tmpa, op=ALU.subtract))
                    dvop(lambda: nc.vector.tensor_tensor(
                        out=mn, in0=tmpa, in1=mn, op=ALU.subtract))
                    dvop(lambda: nc.vector.tensor_tensor(
                        out=mx, in0=mx, in1=mn, op=ALU.max))
                    dvop(lambda: nc.vector.tensor_tensor(
                        out=g1, in0=sp, in1=mx, op=ALU.mult))
                    dvop(lambda: nc.vector.tensor_scalar(
                        out=ng1, in0=g1, scalar1=-1.0, scalar2=None,
                        op0=ALU.mult))
                    dvop(lambda: nc.vector.tensor_scalar(
                        out=nm, in0=tmpa, scalar1=-1.0, scalar2=None,
                        op0=ALU.mult))
                    with nc.psum_tensor(_nm("trg"), [128, 1024], BF16) as trg:
                        for ti in range(TG):
                            gv = g_sb[:, ti * F:(ti + 1) * F]
                            s = ti * 16
                            b.inc(nc.vector.tensor_scalar(
                                out=sq[:, 0:F], in0=gv,
                                scalar1=st1[:, s + 4:s + 5],
                                scalar2=st1[:, s + 5:s + 6],
                                op0=ALU.add, op1=ALU.mult), "DV")
                            b.wait("dve", "DV")
                            b.inc(nc.vector.tensor_scalar(
                                out=gv, in0=sq[:, 0:F],
                                scalar1=st1[:, s + 6:s + 7],
                                scalar2=st1[:, s + 7:s + 8],
                                op0=ALU.min, op1=ALU.max), "DV")
                            b.wait("pe", "DV")
                            for bb in range(cfg.NPSI):
                                b.inc(nc.tensor.transpose(
                                    out=trg[:, (bb % 8) * 128:
                                            (bb % 8) * 128 + 128],
                                    in_=gv[:, bb * 128:bb * 128 + 128],
                                    identity=ident_sb[:]), "PE")
                                if bb % 8 == 7 or bb == cfg.NPSI - 1:
                                    nblk = bb % 8 + 1
                                    c0 = (bb - nblk + 1) * 128
                                    b.wait("dve", "PE")
                                    b.inc(nc.vector.tensor_copy(
                                        out=gv[:, c0:c0 + nblk * 128],
                                        in_=trg[:, 0:nblk * 128]), "DV")
                                    if bb != cfg.NPSI - 1:
                                        b.wait("pe", "DV")
                    with contextlib.ExitStack() as pstk:
                        pps = [pstk.enter_context(nc.psum_tensor(
                            _nm(f"pr{i}"), [128, cfg.OW], F32))
                            for i in range(TG)]
                        ev_done = {}
                        st_done = {}
                        pe_after = {}
                        ld_w = {}
                        ld2a = {}
                        n_it = cfg.OG * cfg.NPSI
                        NSL = len(wfs)

                        def xsl_ld(idx, tt0_, xs_):
                            if idx in ld2a:
                                return
                            oi_, ti_ = idx // TG, idx % TG
                            if idx >= 2:
                                pi = idx - 2
                                b.wait("act", "DV",
                                       ev_done.get((pi // TG, pi % TG), 0))
                            b.inc(nc.scalar.dma_start(
                                out=xss[idx % 2][:],
                                in_=xs_[(tt0_ + ti_) * 128:
                                        (tt0_ + ti_ + 1) * 128,
                                        oi_ * cfg.OW:(oi_ + 1) * cfg.OW]),
                                "LD2", 16)
                            ld2a[idx] = b.cnt["LD2"]

                        def pr_load(it):
                            oi, ps_ = it // cfg.NPSI, it % cfg.NPSI
                            wbuf = wfs[it % NSL]
                            if it >= NSL:
                                b.wait("sync", "PE", pe_after[it - NSL])
                            else:
                                b.wait("sync", "PE")
                            b.inc(nc.sync.dma_start(
                                out=wbuf[:, 0:cfg.OW],
                                in_=projT_b[ps_ * 128:(ps_ + 1) * 128,
                                            oi * cfg.OW:
                                            (oi + 1) * cfg.OW]), "LD", 16)
                            ld_w[it] = b.cnt["LD"]

                        for it0 in range(min(NSL, n_it)):
                            pr_load(it0)
                        b.wait("pe", "DV")
                        pe_fence(pps[0][:, 0:128])
                        for oi in range(cfg.OG):
                            for ps_ in range(cfg.NPSI):
                                it = oi * cfg.NPSI + ps_
                                wbuf = wfs[it % NSL]
                                b.wait("pe", "LD",
                                       ld_w.get(it + 1, ld_w[it]))
                                b.wait("pe", "LD", ld_w[it])
                                b.wait("pe", "DV")
                                for ti in range(TG):
                                    if ps_ == 0 and oi > 0:
                                        b.wait("pe", "DV",
                                               ev_done[(oi - 1, ti)])
                                    b.inc(nc.tensor.matmul(
                                        out=pps[ti][:],
                                        lhsT=g_sb[:, ti * F + ps_ * 128:
                                                  ti * F + ps_ * 128 +
                                                  128],
                                        rhs=wbuf[:, 0:cfg.OW],
                                        start=(ps_ == 0),
                                        stop=(ps_ == cfg.NPSI - 1)), "PE")
                                pe_after[it] = b.cnt["PE"]
                                if it + NSL < n_it:
                                    pr_load(it + NSL)
                                elif it == n_it - 1:
                                    ld_w[it + 1] = b.inc(nc.sync.dma_start(
                                        out=scr_sb[:, 0:cfg.OW],
                                        in_=projT_b[0:128, 0:cfg.OW]), "LD", 16)
                            for ti in range(TG):
                                tau = tt0 + ti
                                idx = oi * TG + ti
                                k = idx % 2
                                xsl, ogl = xss[k], ogs[k]
                                if idx == 0:
                                    xsl_ld(0, tt0, x_src)
                                nxt = idx + 1
                                if nxt < cfg.OG * TG:
                                    xsl_ld(nxt, tt0, x_src)
                                else:
                                    ld2a[nxt] = b.inc(nc.scalar.dma_start(
                                        out=scrg_sb[:, 0:16],
                                        in_=x_src[0:128, 0:16]), "LD2", 16)
                                b.wait("dve", "LD2", ld2a[nxt])
                                b.wait("dve", "PE")
                                if idx >= 2:
                                    b.wait("dve", "STG",
                                           st_done[idx - 2])
                                ev_done[(oi, ti)] = b.inc(
                                    nc.vector.scalar_tensor_tensor(
                                        out=ogl[:], in0=pps[ti][:],
                                        scalar=gl_sb[:, 6:7], in1=xsl[:],
                                        op0=ALU.mult, op1=ALU.add), "DV")
                                b.wait("pool", "DV")
                                st_done[idx] = b.inc(nc.gpsimd.dma_start(
                                    out=x_dst[tau * 128:(tau + 1) * 128,
                                              oi * cfg.OW:
                                              (oi + 1) * cfg.OW],
                                    in_=ogl[:]), "STG", 16)

        def ffn_flush(x_dst):
            b.wait("pool", "STG")
            b.inc(nc.gpsimd.dma_start(
                out=scrf_sb[:, 64:80], in_=x_dst[0:128, 0:16]), "LDG", 16)

        ffn(x0, res1)
        ffn_flush(res1)
        b.gate()

        if only_ffn1:
            for tau in range(n_tt):
                b.inc(nc.gpsimd.dma_start(
                    out=y[tau * 128:(tau + 1) * 128, :],
                    in_=res1[tau * 128:(tau + 1) * 128, :]), "STG", 16)
            b.wait("pool", "STG")
            b.gate(["pool"])
            return nc

        def attention():
            TGW, NTG = cfg.TGW, cfg.NTG
            QT = cfg.qg_tau
            astack = contextlib.ExitStack()
            q_sb = astack.enter_context(
                nc.sbuf_tensor(_nm("q_sb"), [128, NH * T], BF16))
            with (
                nc.sbuf_tensor(_nm("xnT"), [128, ND * T], BF16) as xnT,
                nc.sbuf_tensor(_nm("xinA"), [128, D], F32) as xinA,
                nc.sbuf_tensor(_nm("xinB"), [128, D], F32) as xinB,
                nc.sbuf_tensor(_nm("xnb"), [128, D], BF16) as xnb,
                nc.sbuf_tensor(_nm("stA"), [128, 8 * n_tt], F32) as stA,
                nc.sbuf_tensor(_nm("wqs"), [128, ND * 1024], BF16) as wqs,
                nc.sbuf_tensor(_nm("kvs0"), [128, 512], BF16) as kvs0,
                nc.sbuf_tensor(_nm("kvs1"), [128, 512], BF16) as kvs1,
            ):
                xins = [xinA, xinB]
                kvss = [kvs0, kvs1]

                def svA(k):
                    return stA.rearrange("p (t k) -> p t k", k=8)[
                        :, 0:n_tt, k:k + 1].rearrange("p t k -> p (t k)")

                a_s1, a_s2 = svA(0), svA(1)
                a_nm, a_rs = svA(2), svA(3)
                a_t1, a_t2 = svA(4), svA(5)
                dv_use = []
                ldaA = {}

                def ald(tau):
                    if tau >= 2:
                        b.wait("sync", "DV", dv_use[tau - 2])
                    b.inc(nc.sync.dma_start(
                        out=xins[tau % 2][:],
                        in_=res1[tau * 128:(tau + 1) * 128, :]), "LD", 16)
                    ldaA[tau] = b.cnt["LD"]

                ald(0)
                for tau in range(n_tt):
                    buf = xins[tau % 2]
                    if tau + 1 < n_tt:
                        ald(tau + 1)
                    else:
                        ldaA[tau + 1] = b.inc(nc.sync.dma_start(
                            out=scrf_sb[:, 0:64],
                            in_=res1[0:128, 0:64]), "LD", 16)
                    b.wait("dve", "LD", ldaA[tau + 1])
                    s = tau * 8
                    b.inc(nc.vector.tensor_reduce(
                        out=stA[:, s:s + 1], in_=buf[:], axis=AX.X,
                        op=ALU.add), "DV")
                    dv_use.append(b.inc(nc.vector.scalar_tensor_tensor(
                        out=xnb[:], in0=buf[:], scalar=1.0, in1=buf[:],
                        op0=ALU.mult, op1=ALU.mult,
                        accum_out=stA[:, s + 1:s + 2]), "DV"))
                b.wait("dve", "DV")
                dvop(lambda: nc.vector.tensor_scalar(
                    out=a_t1, in0=a_s1, scalar1=1.0 / D, scalar2=None,
                    op0=ALU.mult))
                dvop(lambda: nc.vector.tensor_scalar(
                    out=a_t2, in0=a_s2, scalar1=1.0 / D, scalar2=None,
                    op0=ALU.mult))
                dvop(lambda: nc.vector.tensor_tensor(
                    out=a_nm, in0=a_t1, in1=a_t1, op=ALU.mult))
                dvop(lambda: nc.vector.tensor_tensor(
                    out=a_t2, in0=a_t2, in1=a_nm, op=ALU.subtract))
                b.wait("act", "DV")
                b.inc(nc.scalar.activation(
                    out=a_rs, in_=a_t2, func=AF.Sqrt, bias=cc_sb[:, 0:1],
                    scale=1.0), "AC")
                b.wait("dve", "AC")
                dvop(lambda: nc.vector.reciprocal(out=a_rs, in_=a_rs))
                dvop(lambda: nc.vector.tensor_scalar(
                    out=a_nm, in0=a_t1, scalar1=-1.0, scalar2=None,
                    op0=ALU.mult))
                with nc.psum_tensor(_nm("trx"), [128, 1024], BF16) as trx:
                    ldaB = {}

                    def ald2(tau):
                        b.wait("sync", "DV")
                        b.inc(nc.sync.dma_start(
                            out=xins[tau % 2][:],
                            in_=res1[tau * 128:(tau + 1) * 128, :]),
                            "LD", 16)
                        ldaB[tau] = b.cnt["LD"]

                    ald2(0)
                    for tau in range(n_tt):
                        buf = xins[tau % 2]
                        if tau + 1 < n_tt:
                            ald2(tau + 1)
                        else:
                            ldaB[tau + 1] = b.inc(nc.sync.dma_start(
                                out=scrf_sb[:, 0:64],
                                in_=res1[0:128, 0:64]), "LD", 16)
                        b.wait("dve", "LD", ldaB[tau + 1])
                        s = tau * 8
                        b.inc(nc.vector.tensor_scalar(
                            out=xnb[:], in0=buf[:],
                            scalar1=stA[:, s + 2:s + 3],
                            scalar2=stA[:, s + 3:s + 4],
                            op0=ALU.add, op1=ALU.mult), "DV")
                        b.wait("pe", "DV")
                        for dd in range(ND):
                            b.inc(nc.tensor.transpose(
                                out=trx[:, (dd % 8) * 128:
                                        (dd % 8) * 128 + 128],
                                in_=xnb[:, dd * 128:dd * 128 + 128],
                                identity=ident_sb[:]), "PE")
                            if dd % 8 == 7 or dd == ND - 1:
                                nblk = dd % 8 + 1
                                d0 = dd - nblk + 1
                                b.wait("dve", "PE")
                                b.inc(nc.vector.tensor_copy(
                                    out=xnT.rearrange(
                                        "p (d t) -> p d t", d=ND)[
                                        :, d0:d0 + nblk,
                                        tau * 128:(tau + 1) * 128],
                                    in_=trx[:, 0:nblk * 128].rearrange(
                                        "p (n c) -> p n c", c=128)), "DV")
                                if dd != ND - 1:
                                    b.wait("pe", "DV")
                n_sl = max(1, cfg.QKVO // 8)
                per_sl = cfg.QKVO // n_sl
                kvst = []
                with contextlib.ExitStack() as pstk:
                    qps = [pstk.enter_context(nc.psum_tensor(
                        _nm(f"qk{i}"), [128, TGW], F32)) for i in range(8)]
                    ev = {}
                    b.wait("pe", "DV")
                    pe_fence(qps[0][:, 0:128])
                    for sl in range(n_sl):
                        if sl >= 1:
                            b.wait("pool", "PE")
                        b.inc(nc.gpsimd.dma_start(
                            out=wqs[:, 0:ND * per_sl * 128].rearrange(
                                "p (d w) -> p d w", d=ND),
                            in_=wqkvT.rearrange(
                                "(d p) o -> p d o", p=128)[
                                :, :, sl * per_sl * 128:
                                (sl + 1) * per_sl * 128]), "LDG", 16)
                        b.inc(nc.gpsimd.dma_start(
                            out=scrg_sb[:], in_=wqkvT[0:128, 0:16]),
                            "LDG", 16)
                        b.wait("pe", "LDG")
                        b.wait("pe", "DV")
                        for wo in range(per_sl):
                            om = sl * per_sl + wo
                            for tg in range(NTG):
                                gi = om * NTG + tg
                                bank = gi % 8
                                if gi >= 8:
                                    prev = gi - 8
                                    if prev in ev:
                                        b.wait("pe", "DV", ev[prev])
                                    if ("a", prev) in ev:
                                        b.wait("pe", "AC",
                                               ev[("a", prev)])
                                for dd in range(ND):
                                    b.inc(nc.tensor.matmul(
                                        out=qps[bank][:],
                                        lhsT=wqs[:, (dd * per_sl + wo) *
                                                 128:
                                                 (dd * per_sl + wo) *
                                                 128 + 128],
                                        rhs=xnT[:, dd * T + tg * TGW:
                                                dd * T + (tg + 1) * TGW],
                                        start=(dd == 0),
                                        stop=(dd == ND - 1)), "PE")
                                if om < NH:
                                    b.wait("dve", "PE")
                                    ev[gi] = b.inc(nc.vector.tensor_copy(
                                        out=q_sb[:, om * T + tg * TGW:
                                                 om * T +
                                                 (tg + 1) * TGW],
                                        in_=qps[bank][:]), "DV")
                                else:
                                    kb = kvss[gi % 2]
                                    b.wait("act", "PE")
                                    if len(kvst) >= 2:
                                        b.wait("act", "STG", kvst[-2])
                                    ev[("a", gi)] = b.inc(
                                        nc.scalar.activation(
                                            out=kb[:, 0:TGW],
                                            in_=qps[bank][:],
                                            func=AF.Identity, bias=0.0,
                                            scale=1.0), "AC")
                                    b.wait("pool", "AC")
                                    r0 = (om - NH) * 128
                                    kvst.append(b.inc(
                                        nc.gpsimd.dma_start(
                                            out=kv_loc[
                                                r0:r0 + 128,
                                                tg * TGW:
                                                (tg + 1) * TGW],
                                            in_=kb[:, 0:TGW]),
                                        "STG", 16))
                b.wait("pool", "STG")
                nchunk = 4
                rows = 2 * NH * 128
                rc = rows // nchunk
                for ci in range(nchunk):
                    b.inc(nc.gpsimd.dma_start(
                        out=kv_sh[bass.ts(myslot, 1),
                                  ci * rc:(ci + 1) * rc, :],
                        in_=kv_loc[ci * rc:(ci + 1) * rc, :].rearrange(
                            "(s r) c -> s r c", s=1)), "STG", 16)
                b.inc(nc.vector.memset(flag_sb[:], FLAG_VAL), "DV")
                b.wait("pool", "DV")
                b.wait("pool", "STG")
                b.inc(nc.gpsimd.dma_start(
                    out=kv_fl[bass.ts(myslot, 1), :], in_=flag_sb[:]),
                    "STG", 16)
                flagreg = nc.gpsimd.alloc_register("flagreg")
                condreg = nc.gpsimd.alloc_register("condreg")
                tgtreg = nc.gpsimd.alloc_register("tgtreg")
                nc.gpsimd.reg_mov(condreg, 1)
                nc.gpsimd.reg_mov(tgtreg, 16)
                with nc.gpsimd.While(condreg):
                    nc.gpsimd.dma_start(
                        out=flin_sb[:],
                        in_=kv_fl[bass.ts(otslot, 1), :]).then_inc(
                        b.sem["FLG"], 16)
                    nc.gpsimd.wait_ge(b.sem["FLG"], tgtreg)
                    nc.gpsimd.reg_alu(tgtreg, tgtreg, 16, ALU.add)
                    nc.gpsimd.load(flagreg, flin_sb[0:1, 0:1])
                    nc.gpsimd.reg_alu(condreg, flagreg, FLAG_VAL,
                                      ALU.not_equal)
                b.inc(nc.gpsimd.engine_nop(), "PL")
            pl_poll = b.cnt["PL"]

            KW = n_tt * 128
            with contextlib.ExitStack() as bstk:
                _e = bstk.enter_context
                k0a = _e(nc.sbuf_tensor(_nm("k0a"), [128, KW], BF16))
                k0b = _e(nc.sbuf_tensor(_nm("k0b"), [128, KW], BF16))
                k1a = _e(nc.sbuf_tensor(_nm("k1a"), [128, KW], BF16))
                k1b = _e(nc.sbuf_tensor(_nm("k1b"), [128, KW], BF16))
                v0a = _e(nc.sbuf_tensor(_nm("v0a"), [128, KW], BF16))
                v0b = _e(nc.sbuf_tensor(_nm("v0b"), [128, KW], BF16))
                v1a = _e(nc.sbuf_tensor(_nm("v1a"), [128, KW], BF16))
                v1b = _e(nc.sbuf_tensor(_nm("v1b"), [128, KW], BF16))
                v_sb = _e(nc.sbuf_tensor(_nm("v_sb"), [128, 2 * KW], BF16))
                prb = _e(nc.sbuf_tensor(_nm("prb"), [128, cfg.S], BF16))
                prT = _e(nc.sbuf_tensor(_nm("prT"),
                                        [128, 2 * n_tt * QT * 128], BF16))
                atn = _e(nc.sbuf_tensor(_nm("atn"), [128, NH * QT * 128],
                                        BF16))
                wos = _e(nc.sbuf_tensor(_nm("wos"), [128, NH * cfg.OW],
                                        BF16))
                stB2 = _e(nc.sbuf_tensor(_nm("stB2"), [128, QT * 32], F32))
                invr = _e(nc.sbuf_tensor(_nm("invr"), [1, QT * 128], F32))
                invbc = _e(nc.sbuf_tensor(_nm("invbc"), [128, QT * 128],
                                          F32))
                xsB0 = _e(nc.sbuf_tensor(_nm("xsB0"), [128, cfg.OW], F32))
                xsB1 = _e(nc.sbuf_tensor(_nm("xsB1"), [128, cfg.OW], F32))
                ogB0 = _e(nc.sbuf_tensor(_nm("ogB0"), [128, cfg.OW], F32))
                ogB1 = _e(nc.sbuf_tensor(_nm("ogB1"), [128, cfg.OW], F32))
                xsBs = [xsB0, xsB1]
                ogBs = [ogB0, ogB1]
                kbufs = [[k0a, k0b], [k1a, k1b], [v0a, v0b], [v1a, v1b]]
                b.wait("sync", "PL", pl_poll)
                b.wait("sync", "STG")
                ost_done = {}
                pairs = [(qg, h) for qg in range(cfg.n_qg)
                         for h in range(NH)]
                ld_after = {}
                pe_after_pair = {}

                def issue_loads(l):
                    qg, h = pairs[l]
                    hm = qg * QT + QT
                    slot = l % 2
                    if l >= 2:
                        b.wait("sync", "PE", pe_after_pair[l - 2])
                    for bi, (r, rbase) in enumerate(
                            ((0, 0), (1, 0), (0, NH), (1, NH))):
                        b.inc(nc.sync.dma_start(
                            out=kbufs[bi][slot][:, 0:hm * 128],
                            in_=kv_sh[r, (rbase + h) * 128:
                                      (rbase + h + 1) * 128,
                                      0:hm * 128]), "LD", 16)
                    ld_after[l] = b.cnt["LD"]

                issue_loads(0)
                for l, (qg, h) in enumerate(pairs):
                    if l + 1 < len(pairs):
                        issue_loads(l + 1)
                    else:
                        ld_after[l + 1] = b.inc(nc.sync.dma_start(
                            out=scr_sb[:, 0:128],
                            in_=kv_sh[0, 0:128, 0:128]), "LD", 16)
                    hm = qg * QT + QT
                    slot = l % 2
                    k0s, k1s = kbufs[0][slot], kbufs[1][slot]
                    v0s, v1s = kbufs[2][slot], kbufs[3][slot]
                    b.wait("pe", "LD", ld_after[l + 1])
                    b.wait("pe", "DV")
                    with nc.psum_tensor(_nm("trv"), [128, 1024],
                                        BF16) as trv:
                        for s in range(2):
                            vsrc = v0s if s == 0 else v1s
                            for j in range(hm):
                                b.inc(nc.tensor.transpose(
                                    out=trv[:, (j % 8) * 128:
                                            (j % 8) * 128 + 128],
                                    in_=vsrc[:, j * 128:j * 128 + 128],
                                    identity=ident_sb[:]), "PE")
                                if j % 8 == 7 or j == hm - 1:
                                    nblk = j % 8 + 1
                                    p0 = s * hm + j - nblk + 1
                                    b.wait("dve", "PE")
                                    b.inc(nc.vector.tensor_copy(
                                        out=v_sb[:, p0 * 128:
                                                 (p0 + nblk) * 128],
                                        in_=trv[:, 0:nblk * 128]), "DV")
                                    if not (s == 1 and j == hm - 1):
                                        b.wait("pe", "DV")
                    b.wait("pool", "PE")
                    for s in range(2):
                        for m in range(qg * QT + 1, hm):
                            inv_n = m - qg * QT
                            pos = s * hm + m
                            b.inc(nc.gpsimd.memset(
                                prT[:, pos * QT * 128:
                                    pos * QT * 128 + inv_n * 128],
                                0.0), "PL")
                    for ti in range(QT):
                        i = qg * QT + ti
                        w = (i + 1) * 128
                        nch = (w + 511) // 512
                        SW = nch * 512
                        sb32 = ti * 32
                        cm1 = nc.psum_tensor(_nm("sc1"), [128, SW], F32)
                        cm0 = nc.psum_tensor(_nm("sc0"), [128, SW], F32)
                        scp1 = cm1.__enter__()
                        scp0 = cm0.__enter__()
                        scps = [scp0, scp1]
                        b.wait("pe", "DV")
                        b.wait("pe", "AC")
                        pe_fence(scp0[:, 0:128])
                        for s in range(2):
                            ks = k0s if s == 0 else k1s
                            scp = scps[s]
                            for c in range(nch):
                                wmm = min(512, w - c * 512)
                                b.inc(nc.tensor.matmul(
                                    out=scp[:, c * 512:c * 512 + wmm],
                                    lhsT=q_sb[:, h * T + i * 128:
                                              h * T + i * 128 + 128],
                                    rhs=ks[:, c * 512:c * 512 + wmm],
                                    start=True, stop=True), "PE")
                            b.wait("dve", "PE")
                            b.inc(nc.vector.tensor_tensor(
                                out=scp[:, w - 128:w],
                                in0=scp[:, w - 128:w],
                                in1=mask_sb[:, s * 128:(s + 1) * 128],
                                op=ALU.add), "DV")
                            b.wait("dve", "DV")
                            b.inc(nc.vector.tensor_reduce(
                                out=stB2[:, sb32 + s:sb32 + s + 1],
                                in_=scp[:, 0:w],
                                axis=AX.X, op=ALU.max), "DV")
                        b.wait("dve", "DV")
                        b.inc(nc.vector.tensor_scalar(
                            out=stB2[:, sb32 + 2:sb32 + 3],
                            in0=stB2[:, sb32:sb32 + 1],
                            scalar1=stB2[:, sb32 + 1:sb32 + 2],
                            scalar2=-cfg.att_c,
                            op0=ALU.max, op1=ALU.mult), "DV")
                        trq_cm = None
                        for s in range(2):
                            scp = scps[s]
                            b.wait("act", "DV")
                            b.wait("act", "PE")
                            b.inc(nc.scalar.activation(
                                out=prb[:, s * SW:s * SW + w],
                                in_=scp[:, 0:w],
                                func=AF.Exp,
                                bias=stB2[:, sb32 + 2:sb32 + 3],
                                scale=cfg.att_c,
                                accum_out=stB2[:, sb32 + 4 + s:
                                               sb32 + 5 + s]), "AC")
                            b.wait("pe", "AC")
                            if s == 0:
                                cm0.__exit__(None, None, None)
                                trq_cm = nc.psum_tensor(
                                    _nm("trq"), [128, 1024], BF16)
                                trq = trq_cm.__enter__()
                            for j in range(i + 1):
                                b.inc(nc.tensor.transpose(
                                    out=trq[:, (j % 8) * 128:
                                            (j % 8) * 128 + 128],
                                    in_=prb[:, s * SW + j * 128:
                                            s * SW + j * 128 + 128],
                                    identity=ident_sb[:]), "PE")
                                if j % 8 == 7 or j == i:
                                    nblk = j % 8 + 1
                                    j0 = j - nblk + 1
                                    p0 = s * hm + j0
                                    b.wait("dve", "PE")
                                    b.inc(nc.vector.tensor_copy(
                                        out=prT.rearrange(
                                            "p (j q) -> p j q",
                                            q=QT * 128)[
                                            :, p0:p0 + nblk,
                                            ti * 128:(ti + 1) * 128],
                                        in_=trq[:, 0:nblk *
                                                128].rearrange(
                                            "p (n c) -> p n c",
                                            c=128)), "DV")
                                    if not (s == 1 and j == i):
                                        b.wait("pe", "DV")
                        trq_cm.__exit__(None, None, None)
                        cm1.__exit__(None, None, None)
                        b.wait("dve", "AC")
                        b.inc(nc.vector.tensor_tensor(
                            out=stB2[:, sb32 + 6:sb32 + 7],
                            in0=stB2[:, sb32 + 4:sb32 + 5],
                            in1=stB2[:, sb32 + 5:sb32 + 6],
                            op=ALU.add), "DV")
                        b.wait("dve", "DV")
                        b.inc(nc.vector.reciprocal(
                            out=stB2[:, sb32 + 7:sb32 + 8],
                            in_=stB2[:, sb32 + 6:sb32 + 7]), "DV")
                    with (
                        nc.psum_tensor(_nm("psI"), [1, QT * 128],
                                       F32) as psI,
                        nc.psum_tensor(_nm("psB"), [128, QT * 128],
                                       F32) as psB,
                    ):
                        b.wait("pe", "DV")
                        for ti in range(QT):
                            b.inc(nc.tensor.transpose(
                                out=psI[:, ti * 128:(ti + 1) * 128],
                                in_=stB2[:, ti * 32 + 7:ti * 32 + 8],
                                identity=identf_sb[:]), "PE")
                        b.wait("dve", "PE")
                        b.inc(nc.vector.tensor_copy(
                            out=invr[:], in_=psI[:]), "DV")
                        b.wait("pe", "DV")
                        pe_fence(psB[:, 0:128])
                        b.inc(nc.tensor.matmul(
                            out=psB[:], lhsT=onesf_sb[:], rhs=invr[:],
                            start=True, stop=True), "PE")
                        b.wait("dve", "PE")
                        b.inc(nc.vector.tensor_copy(
                            out=invbc[:], in_=psB[:]), "DV")
                    with nc.psum_tensor(_nm("pv"), [128, QT * 128],
                                        F32) as pvp:
                        b.wait("pe", "DV")
                        b.wait("pe", "PL")
                        pe_fence(pvp[:, 0:128])
                        for pos in range(2 * hm):
                            b.inc(nc.tensor.matmul(
                                out=pvp[:],
                                lhsT=v_sb[:, pos * 128:pos * 128 + 128],
                                rhs=prT[:, pos * QT * 128:
                                        (pos + 1) * QT * 128],
                                start=(pos == 0),
                                stop=(pos == 2 * hm - 1)), "PE")
                        b.wait("dve", "PE")
                        b.inc(nc.vector.tensor_tensor(
                            out=atn[:, h * QT * 128:(h + 1) * QT * 128],
                            in0=pvp[:], in1=invbc[:], op=ALU.mult), "DV")
                    pe_after_pair[l] = b.cnt["PE"]
                    if h != NH - 1:
                        continue
                    with contextlib.ExitStack() as pstk:
                        ops = [pstk.enter_context(nc.psum_tensor(
                            _nm(f"op{i}"), [128, cfg.OW], F32))
                            for i in range(QT)]
                        for oi in range(cfg.OG):
                            b.wait("pool", "PE")
                            b.inc(nc.gpsimd.dma_start(
                                out=wos.rearrange(
                                    "p (hh o) -> p hh o", hh=NH),
                                in_=woT.rearrange(
                                    "(hh p) o -> p hh o", p=128)[
                                    :, :, oi * cfg.OW:(oi + 1) * cfg.OW]),
                                "LDG", 16)
                            b.inc(nc.gpsimd.dma_start(
                                out=scrg_sb[:], in_=woT[0:128, 0:16]),
                                "LDG", 16)
                            b.wait("pe", "LDG")
                            b.wait("pe", "DV")
                            for hc in range(NH):
                                for ti in range(QT):
                                    b.inc(nc.tensor.matmul(
                                        out=ops[ti][:],
                                        lhsT=atn[:, hc * QT * 128 +
                                                 ti * 128:
                                                 hc * QT * 128 +
                                                 ti * 128 + 128],
                                        rhs=wos[:, hc * cfg.OW:
                                                (hc + 1) * cfg.OW],
                                        start=(hc == 0),
                                        stop=(hc == NH - 1)), "PE")
                            for ti in range(QT):
                                tau = qg * QT + ti
                                idx = (qg * cfg.OG + oi) * QT + ti
                                k = idx % 2
                                xsl, ogl = xsBs[k], ogBs[k]
                                if idx >= 2:
                                    b.wait("act", "DV")
                                b.inc(nc.scalar.dma_start(
                                    out=xsl[:],
                                    in_=res1[tau * 128:(tau + 1) * 128,
                                             oi * cfg.OW:
                                             (oi + 1) * cfg.OW]),
                                    "LD2", 16)
                                b.inc(nc.scalar.dma_start(
                                    out=scrg_sb[:, 0:16],
                                    in_=res1[0:128, 0:16]), "LD2", 16)
                                b.wait("dve", "LD2")
                                b.wait("dve", "PE")
                                if idx >= 2:
                                    b.wait("dve", "STG",
                                           ost_done[idx - 2])
                                b.inc(nc.vector.tensor_tensor(
                                    out=ogl[:], in0=ops[ti][:],
                                    in1=xsl[:], op=ALU.add), "DV")
                                b.wait("pool", "DV")
                                ost_done[idx] = b.inc(
                                    nc.gpsimd.dma_start(
                                        out=res2[tau * 128:
                                                 (tau + 1) * 128,
                                                 oi * cfg.OW:
                                                 (oi + 1) * cfg.OW],
                                        in_=ogl[:]), "STG", 16)
                        pe_after_pair[l] = b.cnt["PE"]
                b.wait("pool", "STG")
                b.inc(nc.gpsimd.dma_start(
                    out=scrf_sb[:, 64:80], in_=res2[0:128, 0:16]),
                    "LDG", 16)
            b.gate()
            astack.close()

        attention()
        b.gate()

        ffn(res2, y)
        ffn_flush(y)

        b.wait("pool", "STG")
        b.gate(["pool"])

    return nc


def _shard_inputs(cfg, x, w_qkv, w_o, w_fc, w_proj):
    T, D = cfg.T, cfg.D
    n_tt = cfg.n_tt
    wqkvT = np.ascontiguousarray(np.asarray(w_qkv, np.float32).T)
    woT = np.ascontiguousarray(np.asarray(w_o, np.float32).T)
    wfcT = np.ascontiguousarray(np.asarray(w_fc, np.float32).T)
    wprojT = np.ascontiguousarray(np.asarray(w_proj, np.float32).T)
    ident = np.eye(128, dtype=ml_dtypes.bfloat16)
    identf = np.eye(128, dtype=np.float32)
    onesf = np.ones((1, 128), np.float32)
    tril = np.where(np.tril(np.ones((128, 128), bool)), 0.0,
                    NEG_BIG).astype(np.float32)
    in_maps = []
    for c in range(8):
        bidx, p = c // 2, c % 2
        xt = np.ascontiguousarray(
            np.asarray(x[bidx], np.float32).reshape(2 * n_tt, 128, D)[p::2]
            .reshape(T, D))
        if p == 0:
            m2 = np.concatenate(
                [tril, np.full((128, 128), NEG_BIG, np.float32)], axis=1)
        else:
            m2 = np.concatenate(
                [np.zeros((128, 128), np.float32), tril], axis=1)
        cconst = np.broadcast_to(
            np.array([1e-5, 1e-6, 0.0, 0.0], np.float32), (128, 4)).copy()
        in_maps.append({
            "x0": xt, "wqkvT": wqkvT, "woT": woT, "wfcT": wfcT,
            "wprojT": wprojT, "ident": ident, "identf": identf,
            "onesf": onesf,
            "mask2": m2, "cconst": cconst,
            "slot": np.array([[p, 1 - p]], np.int32),
        })
    return in_maps


def _gather_output(cfg, results, B_, S_, D_):
    out = np.empty((B_, S_, D_), np.float32)
    for c in range(8):
        bidx, p = c // 2, c % 2
        yc = np.asarray(results[c]["y"], np.float32).reshape(
            cfg.n_tt, 128, D_)
        out[bidx].reshape(2 * cfg.n_tt, 128, D_)[p::2] = yc
    return out


def kernel(L0_attn_hidden_states, L0_attn_w_qkv, L0_attn_w_o,
           L0_ffn_w_fc, L0_ffn_w_proj):
    cfg = FULL
    x = np.asarray(L0_attn_hidden_states, np.float32)
    B_, S_, D_ = x.shape
    in_maps = _shard_inputs(cfg, x, L0_attn_w_qkv, L0_attn_w_o,
                            L0_ffn_w_fc, L0_ffn_w_proj)
    nc = build(cfg)
    res = run_bass_kernel_spmd(nc, in_maps, core_ids=list(range(8)))
    return _gather_output(cfg, res.results, B_, S_, D_)
